# version 15 (final)
# baseline (speedup 1.0000x reference)
import contextlib
import numpy as np
import ml_dtypes
import concourse.bass as bass
import concourse.mybir as mybir
from concourse.alu_op_type import AluOpType as ALU
from concourse.bass_utils import run_bass_kernel_spmd

F32 = mybir.dt.float32
BF16 = mybir.dt.bfloat16
AF = mybir.ActivationFunctionType
BF = ml_dtypes.bfloat16

D = 2048
DI = 4096
DXB = 1024
DST = 16
DCONV = 4
DTR = 128
INTER = 8192
L = 4096
T = 512
NCHUNK = L // T
KD = D // 128
EPS = 1e-5

MT_Z0, MT_X0, MT_B0, MT_C0, MT_DT = 0, 32, 40, 48, 80
DEBUG = False
ONLY_P1 = False
PARANOID = True
ELIDE_ENGINES = ()
P1B = 0

ENGS = ("T", "V", "S", "G")
QPOOL = {"qQ": 10, "qG": 8, "qS": 4}
QUEUES = tuple(f"{q}{i}" for q, n in QPOOL.items() for i in range(n))

WSPEC = {
    "inproj": (D, 10368), "dtproj": (DTR, DI), "outproj": (DI, D),
    "gate": (D, INTER), "up": (D, INTER), "down": (INTER, D),
    "fup": (D, INTER), "fdown": (INTER, D),
}
SIGN_W = ("fup", "fdown")


class Buf:
    __slots__ = ("w", "r", "wl")

    def __init__(self):
        self.w = None
        self.r = []
        self.wl = []


class Sched:
    def __init__(self):
        self.thunks = {e: [] for e in ENGS + ("Q",)}
        self.cnt = {s: 0 for s in ENGS + QUEUES}
        self.qissue = {q: 0 for q in QPOOL}
        self.waited = {e: {} for e in ENGS + ("Q",)}
        self.sems = None
        self.aliased = []

    def _need(self, elide, tok, need):
        sem, val = tok
        if sem == elide:
            return
        if need.get(sem, 0) < val:
            need[sem] = val

    def op(self, e, build, reads=(), writes=(), queue=None, wwrites=(), sreads=()):
        issuer = "Q" if e == "Q" else e
        elide = issuer if (queue is None and (issuer in ELIDE_ENGINES or not PARANOID)) else None
        need = {}
        for b in sreads:
            if b.w:
                self._need(None, b.w, need)
            for dep in b.wl:
                self._need(None, dep, need)
        for b in reads:
            if b.w:
                self._need(elide, b.w, need)
            for dep in b.wl:
                self._need(elide, dep, need)
        for b in writes:
            if b.w:
                self._need(elide, b.w, need)
            for dep in b.wl:
                self._need(elide, dep, need)
            for dep in b.r:
                self._need(elide, dep, need)
        for b in wwrites:
            for dep in b.r:
                self._need(elide, dep, need)
        writes = list(writes) + list(wwrites)
        waits = []
        wcache = self.waited[issuer]
        for sem, val in need.items():
            if wcache.get(sem, 0) < val:
                wcache[sem] = val
                waits.append((sem, val))
        if queue is None:
            self.cnt[e] += 1
            tok = (e, self.cnt[e])
            inc = (e, 1)
        else:
            i = self.qissue[queue]
            self.qissue[queue] = i + 1
            k = QPOOL[queue]
            qsem = f"{queue}{i % k}"
            if need.get(qsem, 0) < 16 * (i // k):
                wcache2 = self.waited[issuer]
                if wcache2.get(qsem, 0) < 16 * (i // k):
                    wcache2[qsem] = 16 * (i // k)
                    waits.append((qsem, 16 * (i // k)))
            self.cnt[qsem] += 16
            tok = (qsem, self.cnt[qsem])
            inc = (qsem, 16)
        for b in reads:
            b.r.append(tok)
        for b in sreads:
            b.r.append(tok)
        nw = len(wwrites)
        for b in writes[:len(writes) - nw]:
            b.w = tok
            b.r = []
            b.wl = []
        for b in writes[len(writes) - nw:]:
            b.w = tok
            b.r = []
            b.wl.append(tok)

        def thunk(h, waits=waits, build=build, inc=inc, self=self):
            for (s, v) in waits:
                h.wait_ge(self.sems[s], v)
            build(h).then_inc(self.sems[inc[0]], inc[1])

        self.thunks[issuer].append(thunk)

    def barrier(self):
        snap = dict(self.cnt)
        qsems = [s for s in QUEUES if not s.startswith("qQ")]
        for e in ENGS:
            waits = []
            wcache = self.waited[e]
            for s in ENGS + tuple(qsems):
                if s != e and wcache.get(s, 0) < snap[s]:
                    wcache[s] = snap[s]
                    waits.append((s, snap[s]))
            self.cnt[e] += 1

            def thunk(h, waits=waits, e=e, self=self):
                for (s, v) in waits:
                    h.wait_ge(self.sems[s], v)
                h.drain().then_inc(self.sems[e], 1)

            self.thunks[e].append(thunk)
        for b in self.aliased:
            b.w = None
            b.r = []
            b.wl = []

    def run(self, e, h):
        for t in self.thunks[e]:
            t(h)


def build_program():
    nc = bass.Bass("TRN2", target_bir_lowering=False)
    sc = Sched()
    wK = {n: i // 128 for n, (i, o) in WSPEC.items()}
    wM = {n: o // 128 for n, (i, o) in WSPEC.items()}

    x_fm = nc.dram_tensor("x_fm", [D, L], F32, kind="ExternalInput").ap()
    dw = {n: nc.dram_tensor("w_" + n, [i, o], F32, kind="ExternalInput").ap()
          for n, (i, o) in WSPEC.items()}
    dpar_in = {}
    for n, shape in [("alog", [128, 32 * DST]), ("convw", [128, 32 * DCONV]),
                     ("convb", [128, 32]), ("dtb", [128, 32]), ("dpar", [128, 32]),
                     ("ln1", [128, KD]), ("ln2", [128, KD]), ("fus", [128, 64]),
                     ("fds", [128, KD]), ("oscale", [1, 1]),
                     ("ones_col", [128, 1]), ("ones_row", [1, 128])]:
        dpar_in[n] = nc.dram_tensor(n, shape, F32, kind="ExternalInput").ap()
    out_fm = nc.dram_tensor("out_fm", [D, L], F32, kind="ExternalOutput").ap()
    wsc = {n: nc.dram_tensor("wsc_" + n, [wM[n], 128, wK[n] * 128], BF16,
                             kind="Internal").ap() for n in WSPEC}
    dbg = {}
    if DEBUG:
        for n, shape, dt_ in [("d_p1", [D, T], F32), ("d_hn", [D, T], BF16),
                              ("d_dtl", [128, T], BF16), ("d_dt0", [128, T], F32),
                              ("d_zs0", [128, T], F32), ("d_cc0", [128, T], BF16),
                              ("d_bo0", [128, T], BF16), ("d_xo0", [128, 515], BF16),
                              ("d_xrp0", [128, 515], BF16), ("d_xcv0", [128, T], F32),
                              ("d_y0", [128, T], BF16), ("d_da0", [128, T], F32),
                              ("d_hh0", [128, T], BF16), ("d_yc0", [128, T], F32),
                              ("d_p4", [D, T], F32), ("d_p5", [D, T], F32),
                              ("d_act0", [D, T], BF16), ("d_ws0", [128, 2048], BF16),
                              ("d_hid0", [128, T], BF16), ("d_pst", [1, 2 * T], F32),
                              ("d_hidall", [128, 64 * T], BF16), ("d_x0", [D, T], F32),
                              ("d_dsos", [128, KD], F32), ("d_oscol", [128, 1], F32)]:
            dbg[n] = nc.dram_tensor(n, shape, dt_, kind="ExternalOutput").ap()
    h0_dram = nc.dram_tensor("h0_dram", [KD, 128, T], F32, kind="Internal").ap()

    es = contextlib.ExitStack()
    sb = lambda name, shape, dt=F32: es.enter_context(nc.sbuf_tensor(name, shape, dt))
    mkpsum = lambda name, shape: es.enter_context(nc.psum_tensor(name, shape, F32))

    with es:
        XRES = sb("XRES", [128, KD * T])
        ACTIN = sb("ACTIN", [128, KD * T], BF16)
        WS = sb("WS", [128, 4 * 2048], BF16)
        DTW = sb("DTW", [128, DI], BF16)
        DTL = sb("DTL", [128, T], BF16)
        STATE = sb("STATE", [128, 512])
        XTL = sb("XTL", [128, 8 * 4], BF16)
        ROWS = sb("ROWS", [1, 3 * T])
        ASB = sb("ASB", [128, 32 * DST])
        CWS = sb("CWS", [128, 32 * DCONV])
        CBS = sb("CBS", [128, 32])
        DTBS = sb("DTBS", [128, 32])
        DPS = sb("DPS", [128, 32])
        LN1S = sb("LN1S", [128, KD])
        LN2S = sb("LN2S", [128, KD])
        FUSS = sb("FUSS", [128, 64])
        DSOS = sb("DSOS", [128, KD])
        OSCS = sb("OSCS", [1, 1])
        EPSC = sb("EPSC", [1, 1])
        OSCOL = sb("OSCOL", [128, 1])
        ONESC = sb("ONESC", [128, 1])
        ONESR = sb("ONESR", [1, 128])
        RB = sb("RB", [128, 34816], BF16)
        RF = sb("RF", [128, 9216])
        PMM = [mkpsum(f"PMM{i}", [128, T]) for i in range(4)]
        PST0 = mkpsum("PST0", [1, T])
        PST1 = mkpsum("PST1", [1, T])
        PB0 = mkpsum("PB0", [128, T])
        PB1 = mkpsum("PB1", [128, T])
        sems = {s: es.enter_context(nc.semaphore("sem_" + s)) for s in ENGS + QUEUES}
        sc.sems = sems

        xres = [Buf() for _ in range(KD)]
        actin = [Buf() for _ in range(KD)]
        wslot = [Buf() for _ in range(4)]
        bDTW, bDTL = Buf(), Buf()
        bSTATE = [Buf() for _ in range(512)]
        bXTL = [Buf() for _ in range(8)]
        rows = [Buf() for _ in range(3)]
        bpmm = [Buf() for _ in range(4)]
        bpst = [Buf(), Buf()]
        bpb = [Buf(), Buf()]
        wscb = {n: Buf() for n in WSPEC}
        h0b = [Buf() for _ in range(KD)]
        consts = Buf()

        def albuf(n):
            bl = [Buf() for _ in range(n)]
            sc.aliased.extend(bl)
            return bl

        hid = albuf(64)
        ybuf = albuf(32)
        xo = albuf(8)
        b4 = albuf(2); cc = albuf(2); bo = albuf(2); xrp = albuf(2)
        da = albuf(4); dbx = albuf(4); hh = albuf(2); hcp = albuf(2)
        cn = albuf(4); bn = albuf(4); gs = albuf(2); ub = albuf(2)
        zs = albuf(2); dtc = albuf(2); dtxc = albuf(2); xcv = albuf(2)
        cacc = albuf(2); spb = albuf(2); tp = albuf(2); h0t = albuf(2)
        hd = albuf(2); sq = albuf(2); yc = albuf(2)
        prew, preb = albuf(1), albuf(1)

        xres_v = lambda dd: XRES[:, dd * T:(dd + 1) * T]
        actin_v = lambda dd: ACTIN[:, dd * T:(dd + 1) * T]
        hid_v = lambda m: RB[:, m * T:(m + 1) * T]
        y_v = lambda c: RB[:, c * T:(c + 1) * T]
        XO0 = 16384
        xo_v = lambda g: RB[:, XO0 + g * 528: XO0 + g * 528 + 515]
        XRP0 = XO0 + 8 * 528
        xrp_v = lambda p: RB[:, XRP0 + p * 528: XRP0 + p * 528 + 515]
        B40 = XRP0 + 2 * 528
        b4_v = lambda p: RB[:, B40 + p * T: B40 + (p + 1) * T]
        CC0 = B40 + 2 * T
        cc_v = lambda p: RB[:, CC0 + p * T: CC0 + (p + 1) * T]
        BO0 = CC0 + 2 * T
        bo_v = lambda p: RB[:, BO0 + p * T: BO0 + (p + 1) * T]
        BN0 = BO0 + 2 * T
        bn_v = lambda p: RB[:, BN0 + p * T: BN0 + (p + 1) * T]
        CN0 = BN0 + 4 * T
        cn_v = lambda p: RB[:, CN0 + p * T: CN0 + (p + 1) * T]
        DBX0 = CN0 + 4 * T
        dbx_v = lambda p: RB[:, DBX0 + p * T: DBX0 + (p + 1) * T]
        HH0 = DBX0 + 4 * T
        hh_v = lambda p: RB[:, HH0 + p * T: HH0 + (p + 1) * T]
        HCP0 = HH0 + 2 * T
        hcp_v = lambda p: RB[:, HCP0 + p * T: HCP0 + (p + 1) * T]
        GS0 = 32768
        gs_v = lambda p: RB[:, GS0 + p * T: GS0 + (p + 1) * T]
        UB0 = GS0 + 2 * T
        ub_v = lambda p: RB[:, UB0 + p * T: UB0 + (p + 1) * T]
        preb_v = RB[:, 0:8192]
        zs_v = lambda p: RF[:, p * T:(p + 1) * T]
        dt_v = lambda p: RF[:, 1024 + p * T: 1024 + (p + 1) * T]
        dtx_v = lambda p: RF[:, 2048 + p * T: 2048 + (p + 1) * T]
        xcv_v = lambda p: RF[:, 3072 + p * T: 3072 + (p + 1) * T]
        cacc_v = lambda p: RF[:, 4096 + p * T: 4096 + (p + 1) * T]
        da_v = lambda p: RF[:, 5120 + p * T: 5120 + (p + 1) * T]
        yc_v = lambda p: RF[:, 7168 + p * T: 7168 + (p + 1) * T]
        sp_v = lambda p: RF[:, 8192 + p * T: 8192 + (p + 1) * T]
        tp_v = lambda p: RF[:, p * T:(p + 1) * T]
        sq_v = lambda p: RF[:, 1024 + p * T: 1024 + (p + 1) * T]
        h0t_v = lambda p: RF[:, 2048 + p * T: 2048 + (p + 1) * T]
        hd_v = lambda p: RF[:, 3072 + p * T: 3072 + (p + 1) * T]
        pre_v = RF[:, 0:8192]
        row_mean = ROWS[0:1, 0:T]
        row_rstd = ROWS[0:1, T:2 * T]
        row_tmp = ROWS[0:1, 2 * T:3 * T]

        op = sc.op
        shuf_masks = [[(i // 16) * 16 + n for i in range(32)] for n in range(DST)]

        for n, dst, bufw in [
            ("alog", ASB, consts), ("convw", CWS, consts), ("convb", CBS, consts),
            ("dtb", DTBS, consts), ("dpar", DPS, consts), ("ln1", LN1S, consts),
            ("ln2", LN2S, consts), ("fus", FUSS, consts), ("fds", DSOS, consts),
            ("oscale", OSCS, consts), ("ones_col", ONESC, consts),
            ("ones_row", ONESR, consts),
        ]:
            op("Q", lambda h, dst=dst, src=dpar_in[n]: h.dma_start(out=dst[:, :], in_=src[:, :]),
               wwrites=[consts], queue="qQ")
        op("S", lambda h: h.activation(out=ASB[:, :], in_=ASB[:, :], func=AF.Exp),
           reads=[consts], writes=[consts])
        op("V", lambda h: h.tensor_scalar(out=ASB[:, :], in0=ASB[:, :], scalar1=-1.0,
                                          scalar2=None, op0=ALU.mult),
           reads=[consts], writes=[consts])
        op("T", lambda h: h.matmul(PB0[:, 0:1], ONESR[0:1, :], OSCS[0:1, 0:1],
                                   start=True, stop=True),
           reads=[consts], writes=[bpb[0]])
        op("V", lambda h: h.tensor_copy(out=OSCOL[:, :], in_=PB0[:, 0:1]),
           reads=[bpb[0]], wwrites=[consts])
        op("V", lambda h: h.tensor_scalar(out=DSOS[:, :], in0=DSOS[:, :],
                                          scalar1=OSCOL[:, 0:1], scalar2=0.5,
                                          op0=ALU.mult, op1=ALU.mult),
           sreads=[consts], writes=[consts])
        if DEBUG:
            op("Q", lambda h: h.dma_start(out=dbg["d_dsos"][:, :], in_=DSOS[:, :]),
               reads=[consts], queue="qQ")
            op("Q", lambda h: h.dma_start(out=dbg["d_oscol"][:, :], in_=OSCOL[:, :]),
               reads=[consts], queue="qQ")
        op("V", lambda h: h.memset(STATE[:, :], 0.0), writes=list(bSTATE))
        op("V", lambda h: h.memset(EPSC[:, :], EPS), wwrites=[consts])
        op("V", lambda h: h.memset(XTL[:, :], 0.0), writes=list(bXTL))
        for dd in range(KD):
            op("Q", lambda h, dd=dd: h.dma_start(out=xres_v(dd), in_=x_fm[dd * 128:(dd + 1) * 128, 0:T]),
               writes=[xres[dd]], queue="qQ")

        for name in ("fup", "fdown", "inproj", "dtproj", "outproj", "gate", "up", "down"):
            K, M = wK[name], wM[name]
            G = max(1, min(M, 8192 // (K * 128)))
            for m0 in range(0, M, G):
                g_n = min(G, M - m0)
                w_ = dw[name]
                for k in range(K):
                    op("Q", lambda h, k=k, m0=m0, g_n=g_n, w_=w_, K=K:
                       h.dma_start(out=pre_v[:, k * g_n * 128:(k + 1) * g_n * 128],
                                   in_=w_[k * 128:(k + 1) * 128, m0 * 128:(m0 + g_n) * 128]),
                       wwrites=[prew[0]], queue="qQ")
                ncols = K * g_n * 128
                if name in SIGN_W:
                    op("S", lambda h, ncols=ncols: h.activation(
                        out=preb_v[:, 0:ncols], in_=pre_v[:, 0:ncols], func=AF.Sign),
                        reads=[prew[0]], writes=[preb[0]])
                else:
                    op("V", lambda h, ncols=ncols: h.tensor_copy(
                        out=preb_v[:, 0:ncols], in_=pre_v[:, 0:ncols]),
                        reads=[prew[0]], writes=[preb[0]])
                src = preb_v[:, 0:ncols].rearrange("p (k mg cc) -> p k mg cc", k=K, mg=g_n, cc=128)
                dst = wsc[name][m0:m0 + g_n, :, :].rearrange("mg p (k cc) -> p k mg cc", k=K, cc=128)
                op("S", lambda h, src=src, dst=dst: h.dma_start(out=dst, in_=src),
                   reads=[preb[0]], writes=[wscb[name]], queue="qS")
        sc.barrier()

        ring = [0]
        pmm_ring = [0]

        def mm_tile(name, m, moving, extra_reads=()):
            K = wK[name]
            assert len(moving) == K
            pi = pmm_ring[0]
            pmm_ring[0] = (pi + 1) % 4
            k = 0
            for j in range((K + 15) // 16):
                sl = ring[0]
                ring[0] = (sl + 1) % 4
                k0 = j * 16
                kn = min(16, K - k0)
                view = WS[:, sl * 2048: sl * 2048 + kn * 128]
                src = wsc[name][m:m + 1, :, k0 * 128:(k0 + kn) * 128]
                op("Q", lambda h, view=view, src=src: h.dma_start(out=view, in_=src),
                   reads=[wscb[name]], writes=[wslot[sl]], queue="qQ")
                for kk in range(kn):
                    lhsT = view[:, kk * 128:(kk + 1) * 128]
                    rhs, rbuf = moving[k]
                    st_ = (k == 0)
                    sp_ = (k == K - 1)
                    wr = [bpmm[pi]] if (st_ or sp_) else []
                    op("T", lambda h, lhsT=lhsT, rhs=rhs, pi=pi, st_=st_, sp_=sp_:
                       h.matmul(PMM[pi][:, :], lhsT, rhs, start=st_, stop=sp_),
                       reads=[wslot[sl], rbuf] + list(extra_reads), writes=wr)
                    k += 1
            return PMM[pi], bpmm[pi]

        def actin_moving():
            return [(actin_v(dd), actin[dd]) for dd in range(KD)]

        def ln_stats(views_bufs, which=0):
            nsrc = len(views_bufs)
            for i, (v, b) in enumerate(views_bufs):
                p = i % 2
                op("S", lambda h, v=v, p=p: h.square(out=sq_v(p), in_=v), reads=[b], writes=[sq[p]])
                op("T", lambda h, v=v, st_=(i == 0), sp_=(i == nsrc - 1):
                   h.matmul(PST0[:, :], ONESC[:, 0:1], v, start=st_, stop=sp_),
                   reads=[b, consts], writes=[bpst[0]])
                op("T", lambda h, p=p, st_=(i == 0), sp_=(i == nsrc - 1):
                   h.matmul(PST1[:, :], ONESC[:, 0:1], sq_v(p), start=st_, stop=sp_),
                   reads=[sq[p], consts], writes=[bpst[1]])

        def rows_layernorm():
            op("V", lambda h: h.tensor_scalar(out=row_mean, in0=PST0[:, :], scalar1=1.0 / D,
                                              scalar2=None, op0=ALU.mult),
               reads=[bpst[0]], writes=[rows[0]])
            op("V", lambda h: h.tensor_scalar(out=row_tmp, in0=PST1[:, :], scalar1=1.0 / D,
                                              scalar2=None, op0=ALU.mult),
               reads=[bpst[1]], writes=[rows[2]])
            op("V", lambda h: h.tensor_tensor(out=row_rstd, in0=row_mean, in1=row_mean, op=ALU.mult),
               sreads=[rows[0]], writes=[rows[1]])
            op("V", lambda h: h.tensor_tensor(out=row_tmp, in0=row_tmp, in1=row_rstd, op=ALU.subtract),
               sreads=[rows[2], rows[1]], writes=[rows[2]])
            op("S", lambda h: h.activation(out=row_rstd, in_=row_tmp, func=AF.Sqrt, bias=EPSC[0:1, 0:1]),
               sreads=[rows[2], consts], writes=[rows[1]])
            op("V", lambda h: h.reciprocal(out=row_rstd, in_=row_rstd),
               sreads=[rows[1]], writes=[rows[1]])

        def rows_rms_only():
            op("V", lambda h: h.tensor_scalar(out=row_tmp, in0=PST1[:, :], scalar1=1.0 / D,
                                              scalar2=None, op0=ALU.mult),
               reads=[bpst[1]], writes=[rows[2]])
            op("S", lambda h: h.activation(out=row_rstd, in_=row_tmp, func=AF.Sqrt, bias=EPSC[0:1, 0:1]),
               sreads=[rows[2], consts], writes=[rows[1]])
            op("V", lambda h: h.reciprocal(out=row_rstd, in_=row_rstd),
               sreads=[rows[1]], writes=[rows[1]])

        def bcast(psum_t, psum_b, row_ap, row_b):
            op("T", lambda h: h.matmul(psum_t[:, :], ONESR[0:1, :], row_ap, start=True, stop=True),
               sreads=[row_b, consts], writes=[psum_b])

        def tap(name, src_ap, src_bufs, rows=None):
            if not DEBUG or name not in dbg:
                return
            dsts = dbg[name]
            if rows is None:
                op("Q", lambda h, dsts=dsts, src_ap=src_ap: h.dma_start(out=dsts[:, :], in_=src_ap),
                   reads=list(src_bufs), queue="qQ")
            else:
                for dd in range(rows):
                    op("Q", lambda h, dd=dd, dsts=dsts: h.dma_start(
                        out=dsts[dd * 128:(dd + 1) * 128, :], in_=src_ap(dd)),
                        reads=[src_bufs[dd]], queue="qQ")

        for u in range(NCHUNK):
            t0 = u * T

            def ffn_phase():
                ln_stats([(xres_v(dd), xres[dd]) for dd in range(KD)])
                rows_layernorm()
                bcast(PB0, bpb[0], row_mean, rows[0])
                bcast(PB1, bpb[1], row_rstd, rows[1])
                if P1B & 1:
                    sc.barrier()
                for dd in range(KD):
                    p = dd % 2
                    op("V", lambda h, dd=dd, p=p: h.tensor_tensor(
                        out=tp_v(p), in0=xres_v(dd), in1=PB0[:, :], op=ALU.subtract),
                        reads=[xres[dd], bpb[0]], writes=[tp[p]])
                    op("V", lambda h, dd=dd, p=p: h.tensor_tensor(
                        out=actin_v(dd), in0=tp_v(p), in1=PB1[:, :], op=ALU.mult),
                        reads=[tp[p], bpb[1]], writes=[actin[dd]])
                if P1B & 2:
                    sc.barrier()
                if u == 0 and not ffn_phase.tapped:
                    ffn_phase.tapped = True
                    tap("d_act0", actin_v, actin, rows=KD)
                    op("V", lambda h: h.tensor_copy(out=RF[:, 7680:8704], in_=ROWS[0:1, 0:1024].rearrange("o (a t) -> o a t", a=2)) if False else h.nop(), reads=[])
                for m in range(64):
                    ps, pbuf = mm_tile("fup", m, actin_moving())
                    if u == 0 and m == 0 and ffn_phase.tapped == True:
                        ffn_phase.tapped = "done"
                        op("Q", lambda h: h.dma_start(out=dbg["d_ws0"][:, :], in_=WS[:, 0:2048]),
                           reads=[wslot[0]], queue="qQ") if DEBUG else None
                    op("S", lambda h, m=m, ps=ps: h.activation(
                        out=hid_v(m), in_=ps[:, :], func=AF.Relu, scale=FUSS[:, m:m + 1]),
                        reads=[pbuf], sreads=[consts], writes=[hid[m]])
                    if u == 0 and m == 0 and DEBUG:
                        tap("d_hid0", hid_v(0), [hid[0]])
                if P1B & 4:
                    sc.barrier()
                if u == 0 and DEBUG and ffn_phase.tapped == "done":
                    ffn_phase.tapped = "done2"
                    for mj in range(64):
                        op("Q", lambda h, mj=mj: h.dma_start(
                            out=dbg["d_hidall"][:, mj * T:(mj + 1) * T], in_=hid_v(mj)),
                            reads=[hid[mj]], queue="qQ")
                for m in range(KD):
                    ps, pbuf = mm_tile("fdown", m, [(hid_v(j), hid[j]) for j in range(64)])
                    op("V", lambda h, m=m, ps=ps: h.scalar_tensor_tensor(
                        out=xres_v(m), in0=ps[:, :], scalar=DSOS[:, m:m + 1], in1=xres_v(m),
                        op0=ALU.mult, op1=ALU.add),
                        reads=[pbuf, xres[m]], sreads=[consts], writes=[xres[m]])

            if u == 0:
                tap("d_x0", xres_v, xres, rows=KD)
            ffn_phase.tapped = False
            ffn_phase()
            if u == 0:
                tap("d_p1", xres_v, xres, rows=KD)
            if ONLY_P1:
                for dd in range(KD):
                    op("Q", lambda h, dd=dd, t0=t0: h.dma_start(
                        out=out_fm[dd * 128:(dd + 1) * 128, t0:t0 + T], in_=xres_v(dd)),
                        reads=[xres[dd]], queue="qQ")
                if u + 1 < NCHUNK:
                    for dd in range(KD):
                        op("Q", lambda h, dd=dd, u=u: h.dma_start(
                            out=xres_v(dd), in_=x_fm[dd * 128:(dd + 1) * 128, (u + 1) * T:(u + 2) * T]),
                            writes=[xres[dd]], queue="qQ")
                sc.barrier()
                continue

            ln_stats([(xres_v(dd), xres[dd]) for dd in range(KD)])
            rows_layernorm()
            bcast(PB0, bpb[0], row_mean, rows[0])
            bcast(PB1, bpb[1], row_rstd, rows[1])
            for dd in range(KD):
                p = dd % 2
                op("V", lambda h, dd=dd, p=p: h.tensor_tensor(
                    out=tp_v(p), in0=xres_v(dd), in1=PB0[:, :], op=ALU.subtract),
                    reads=[xres[dd], bpb[0]], writes=[tp[p]])
                op("V", lambda h, p=p: h.tensor_tensor(
                    out=h0t_v(p), in0=tp_v(p), in1=PB1[:, :], op=ALU.mult),
                    reads=[tp[p], bpb[1]], writes=[h0t[p]])
                op("S", lambda h, p=p, dd=dd: h.dma_start(
                    out=h0_dram[dd:dd + 1, :, :], in_=h0t_v(p)),
                    reads=[h0t[p]], writes=[h0b[dd]], queue="qS")
                op("S", lambda h, p=p: h.square(out=sq_v(p), in_=h0t_v(p)),
                   reads=[h0t[p]], writes=[sq[p]])
                op("T", lambda h, p=p, st_=(dd == 0), sp_=(dd == KD - 1):
                   h.matmul(PST1[:, :], ONESC[:, 0:1], sq_v(p), start=st_, stop=sp_),
                   reads=[sq[p], consts], writes=[bpst[1]])
            op("V", lambda h: h.tensor_scalar(out=row_tmp, in0=PST1[:, :], scalar1=1.0 / D,
                                              scalar2=None, op0=ALU.mult),
               reads=[bpst[1]], writes=[rows[2]])
            op("S", lambda h: h.activation(out=row_tmp, in_=row_tmp, func=AF.Sqrt, bias=EPSC[0:1, 0:1]),
               sreads=[rows[2], consts], writes=[rows[2]])
            op("V", lambda h: h.reciprocal(out=row_tmp, in_=row_tmp),
               sreads=[rows[2]], writes=[rows[2]])
            op("V", lambda h: h.tensor_tensor(out=row_tmp, in0=row_tmp, in1=row_rstd, op=ALU.mult),
               sreads=[rows[2], rows[1]], writes=[rows[2]])
            bcast(PB1, bpb[1], row_tmp, rows[2])
            for dd in range(KD):
                p = dd % 2
                op("V", lambda h, dd=dd, p=p: h.tensor_tensor(
                    out=tp_v(p), in0=xres_v(dd), in1=PB0[:, :], op=ALU.subtract),
                    reads=[xres[dd], bpb[0]], writes=[tp[p]])
                op("V", lambda h, dd=dd, p=p: h.scalar_tensor_tensor(
                    out=actin_v(dd), in0=PB1[:, :], scalar=LN1S[:, dd:dd + 1], in1=tp_v(p),
                    op0=ALU.mult, op1=ALU.mult),
                    reads=[bpb[1], tp[p]], sreads=[consts], writes=[actin[dd]])

            if u == 0:
                tap("d_hn", actin_v, actin, rows=KD)
            sc.barrier()

            op("Q", lambda h: h.dma_start(
                out=DTW[:, :].rearrange("p (mg cc) -> p mg cc", mg=32, cc=128),
                in_=wsc["dtproj"].rearrange("mg p cc -> p mg cc")),
                reads=[wscb["dtproj"]], writes=[bDTW], queue="qQ")
            ps, pbuf = mm_tile("inproj", MT_DT, actin_moving())
            op("S", lambda h, ps=ps: h.activation(out=DTL[:, :], in_=ps[:, :], func=AF.Copy),
               reads=[pbuf], writes=[bDTL])
            if u == 0:
                tap("d_dtl", DTL[:, :], [bDTL])

            for c in range(32):
                pc = c % 2
                g = c // 4
                if c % 4 == 0:
                    pg = g % 2
                    ps, pbuf = mm_tile("inproj", MT_X0 + g, actin_moving())
                    op("S", lambda h, g=g, ps=ps: h.activation(
                        out=RB[:, XO0 + g * 528 + 3: XO0 + g * 528 + 515],
                        in_=ps[:, :], func=AF.Copy),
                        reads=[pbuf], writes=[xo[g]])
                    op("V", lambda h, g=g: h.tensor_copy(
                        out=RB[:, XO0 + g * 528: XO0 + g * 528 + 3],
                        in_=XTL[:, g * 4: g * 4 + 3]),
                        reads=[bXTL[g]], writes=[xo[g]])
                    ps, pbuf = mm_tile("inproj", MT_B0 + g, actin_moving())
                    op("S", lambda h, pg=pg, ps=ps: h.activation(
                        out=bo_v(pg), in_=ps[:, :], func=AF.Copy),
                        reads=[pbuf], writes=[bo[pg]])
                pg = g % 2
                ps, pbuf = mm_tile("inproj", MT_Z0 + c, actin_moving())
                op("S", lambda h, pc=pc, ps=ps: h.activation(
                    out=zs_v(pc), in_=ps[:, :], func=AF.Silu),
                    reads=[pbuf], writes=[zs[pc]])
                ps, pbuf = mm_tile("inproj", MT_C0 + c, actin_moving())
                op("S", lambda h, pc=pc, ps=ps: h.activation(
                    out=cc_v(pc), in_=ps[:, :], func=AF.Copy),
                    reads=[pbuf], writes=[cc[pc]])
                pi = pmm_ring[0]
                pmm_ring[0] = (pi + 1) % 4
                op("T", lambda h, c=c, pi=pi: h.matmul(
                    PMM[pi][:, :], DTW[:, c * 128:(c + 1) * 128], DTL[:, :],
                    start=True, stop=True),
                    reads=[bDTW, bDTL], writes=[bpmm[pi]])
                op("S", lambda h, c=c, pi=pi: h.activation(
                    out=sp_v(0), in_=PMM[pi][:, :], func=AF.Abs, bias=DTBS[:, c:c + 1]),
                    reads=[bpmm[pi]], sreads=[consts], writes=[spb[0]])
                op("S", lambda h: h.activation(out=sp_v(0), in_=sp_v(0), func=AF.Exp, scale=-1.0),
                   reads=[spb[0]], writes=[spb[0]])
                op("S", lambda h: h.activation(out=sp_v(0), in_=sp_v(0), func=AF.Ln, bias=1.0),
                   reads=[spb[0]], writes=[spb[0]])
                op("S", lambda h, c=c, pi=pi: h.activation(
                    out=sp_v(1), in_=PMM[pi][:, :], func=AF.Relu, bias=DTBS[:, c:c + 1]),
                    reads=[bpmm[pi]], sreads=[consts], writes=[spb[1]])
                op("V", lambda h, pc=pc: h.tensor_tensor(
                    out=dt_v(pc), in0=sp_v(0), in1=sp_v(1), op=ALU.add),
                    reads=[spb[0], spb[1]], writes=[dtc[pc]])
                if u == 0 and c == 0:
                    tap("d_dt0", dt_v(pc), [dtc[pc]])
                    tap("d_zs0", zs_v(pc), [zs[pc]])
                    tap("d_cc0", cc_v(pc), [cc[pc]])
                    tap("d_bo0", bo_v(pg), [bo[pg]])
                    tap("d_xo0", xo_v(0), [xo[0]])
                for b_ in range(4):
                    for a_ in range(2):
                        dsl = (64 * a_ + 16 * b_, 64 * a_ + 16 * b_ + 16)
                        ssl = (32 * (c % 4) + 16 * a_, 32 * (c % 4) + 16 * a_ + 16)
                        op("Q", lambda h, dsl=dsl, ssl=ssl, g=g, pc=pc: h.dma_start(
                            out=xrp_v(pc)[dsl[0]:dsl[1], :],
                            in_=xo_v(g)[ssl[0]:ssl[1], :]),
                            reads=[xo[g]], writes=[xrp[pc]], queue="qG")
                        op("Q", lambda h, dsl=dsl, ssl=ssl, pg=pg, pc=pc: h.dma_start(
                            out=b4_v(pc)[dsl[0]:dsl[1], 0:512],
                            in_=bo_v(pg)[ssl[0]:ssl[1], 0:512]),
                            reads=[bo[pg]], writes=[b4[pc]], queue="qG")
                op("V", lambda h, c=c, pc=pc: h.tensor_scalar(
                    out=cacc_v(0), in0=xrp_v(pc)[:, 0:T], scalar1=CWS[:, c * 4: c * 4 + 1],
                    scalar2=None, op0=ALU.mult),
                    reads=[xrp[pc]], sreads=[consts], writes=[cacc[0]])
                for j in (1, 2, 3):
                    src_p, dst_p = (j + 1) % 2, j % 2
                    op("V", lambda h, c=c, j=j, pc=pc, src_p=src_p, dst_p=dst_p:
                       h.scalar_tensor_tensor(
                           out=cacc_v(dst_p), in0=xrp_v(pc)[:, j: j + T],
                           scalar=CWS[:, c * 4 + j: c * 4 + j + 1], in1=cacc_v(src_p),
                           op0=ALU.mult, op1=ALU.add),
                       reads=[xrp[pc], cacc[src_p]], sreads=[consts], writes=[cacc[dst_p]])
                op("S", lambda h, c=c, pc=pc: h.activation(
                    out=xcv_v(pc), in_=cacc_v(1), func=AF.Silu, bias=CBS[:, c:c + 1]),
                    reads=[cacc[1]], sreads=[consts], writes=[xcv[pc]])
                if u == 0 and c == 0:
                    tap("d_xrp0", xrp_v(pc), [xrp[pc]])
                    tap("d_xcv0", xcv_v(pc), [xcv[pc]])
                op("G", lambda h, pc=pc: h.tensor_tensor(
                    out=dtx_v(pc), in0=dt_v(pc), in1=xcv_v(pc), op=ALU.mult),
                    reads=[dtc[pc], xcv[pc]], writes=[dtxc[pc]])
                LA = 2
                for j in range(DST + LA):
                    if j < DST:
                        n, p4 = j, j % 4
                        col = c * 16 + n
                        op("S", lambda h, pc=pc, p4=p4, col=col: h.activation(
                            out=da_v(p4), in_=dt_v(pc), func=AF.Exp, scale=ASB[:, col:col + 1]),
                            reads=[dtc[pc]], sreads=[consts], writes=[da[p4]])
                        op("V", lambda h, pc=pc, p4=p4, n=n: h.stream_shuffle(
                            out=bn_v(p4), in_=b4_v(pc), mask=shuf_masks[n]),
                            reads=[b4[pc]], writes=[bn[p4]])
                        op("V", lambda h, pc=pc, p4=p4, n=n: h.stream_shuffle(
                            out=cn_v(p4), in_=cc_v(pc), mask=shuf_masks[n]),
                            reads=[cc[pc]], writes=[cn[p4]])
                        op("G", lambda h, pc=pc, p4=p4: h.tensor_tensor(
                            out=dbx_v(p4), in0=dtx_v(pc), in1=bn_v(p4), op=ALU.mult),
                            reads=[dtxc[pc], bn[p4]], writes=[dbx[p4]])
                    n = j - LA
                    if 0 <= n < DST:
                        pn, p4 = n % 2, n % 4
                        col = c * 16 + n
                        op("V", lambda h, pn=pn, p4=p4, col=col: h.tensor_tensor_scan(
                            out=hh_v(pn), data0=da_v(p4), data1=dbx_v(p4),
                            initial=STATE[:, col:col + 1], op0=ALU.mult, op1=ALU.add),
                            reads=[da[p4], dbx[p4]], sreads=[bSTATE[col]], writes=[hh[pn]])
                        op("S", lambda h, pn=pn, col=col: h.copy(
                            out=STATE[:, col:col + 1], in_=hh_v(pn)[:, T - 1:T]),
                            reads=[hh[pn]], writes=[bSTATE[col]])
                        op("V", lambda h, pn=pn, p4=p4: h.tensor_tensor(
                            out=hcp_v(pn), in0=hh_v(pn), in1=cn_v(p4), op=ALU.mult),
                            reads=[hh[pn], cn[p4]], writes=[hcp[pn]])
                        if n == 0:
                            op("G", lambda h, pn=pn, pc=pc: h.tensor_copy(
                                out=yc_v(pc), in_=hcp_v(pn)),
                                reads=[hcp[pn]], writes=[yc[pc]])
                        else:
                            op("G", lambda h, pn=pn, pc=pc: h.tensor_tensor(
                                out=yc_v(pc), in0=hcp_v(pn), in1=yc_v(pc), op=ALU.add),
                                reads=[hcp[pn], yc[pc]], writes=[yc[pc]])
                op("V", lambda h, c=c, pc=pc: h.scalar_tensor_tensor(
                    out=yc_v(pc), in0=xcv_v(pc), scalar=DPS[:, c:c + 1], in1=yc_v(pc),
                    op0=ALU.mult, op1=ALU.add),
                    reads=[xcv[pc], yc[pc]], sreads=[consts], writes=[yc[pc]])
                if u == 0 and c == 0:
                    tap("d_yc0", yc_v(pc), [yc[pc]])
                op("V", lambda h, c=c, pc=pc: h.tensor_tensor(
                    out=y_v(c), in0=yc_v(pc), in1=zs_v(pc), op=ALU.mult),
                    reads=[yc[pc], zs[pc]], writes=[ybuf[c]])
                if u == 0 and c == 0:
                    tap("d_y0", y_v(c), [ybuf[c]])
            for g in range(8):
                op("V", lambda h, g=g: h.tensor_copy(
                    out=XTL[:, g * 4: g * 4 + 3],
                    in_=RB[:, XO0 + g * 528 + 512: XO0 + g * 528 + 515]),
                    reads=[xo[g]], writes=[bXTL[g]])

            sc.barrier()

            for m in range(KD):
                p = m % 2
                op("S", lambda h, m=m, p=p: h.dma_start(
                    out=h0t_v(p), in_=h0_dram[m:m + 1, :, :]),
                    reads=[h0b[m]], writes=[h0t[p]], queue="qS")
                ps, pbuf = mm_tile("outproj", m, [(y_v(c), ybuf[c]) for c in range(32)])
                op("V", lambda h, p=p, ps=ps: h.tensor_tensor(
                    out=hd_v(p), in0=ps[:, :], in1=h0t_v(p), op=ALU.add),
                    reads=[pbuf, h0t[p]], writes=[hd[p]])
                op("V", lambda h, m=m, p=p: h.tensor_tensor(
                    out=xres_v(m), in0=hd_v(p), in1=xres_v(m), op=ALU.add),
                    reads=[hd[p], xres[m]], writes=[xres[m]])
                op("S", lambda h, m=m, p=p: h.activation(
                    out=actin_v(m), in_=hd_v(p), func=AF.Copy),
                    reads=[hd[p]], writes=[actin[m]])
                op("S", lambda h, p=p: h.square(out=sq_v(p), in_=hd_v(p)),
                   reads=[hd[p]], writes=[sq[p]])
                op("T", lambda h, p=p, st_=(m == 0), sp_=(m == KD - 1):
                   h.matmul(PST1[:, :], ONESC[:, 0:1], sq_v(p), start=st_, stop=sp_),
                   reads=[sq[p], consts], writes=[bpst[1]])
            rows_rms_only()
            bcast(PB0, bpb[0], row_rstd, rows[1])
            for dd in range(KD):
                op("V", lambda h, dd=dd: h.scalar_tensor_tensor(
                    out=actin_v(dd), in0=PB0[:, :], scalar=LN2S[:, dd:dd + 1], in1=actin_v(dd),
                    op0=ALU.mult, op1=ALU.mult),
                    reads=[bpb[0], actin[dd]], sreads=[consts], writes=[actin[dd]])

            if u == 0:
                tap("d_p4", xres_v, xres, rows=KD)
            sc.barrier()

            for m in range(64):
                p = m % 2
                psg, pbg = mm_tile("gate", m, actin_moving())
                op("S", lambda h, p=p, psg=psg: h.activation(
                    out=gs_v(p), in_=psg[:, :], func=AF.Silu),
                    reads=[pbg], writes=[gs[p]])
                psu, pbu = mm_tile("up", m, actin_moving())
                op("S", lambda h, p=p, psu=psu: h.activation(
                    out=ub_v(p), in_=psu[:, :], func=AF.Copy),
                    reads=[pbu], writes=[ub[p]])
                op("V", lambda h, m=m, p=p: h.tensor_tensor(
                    out=hid_v(m), in0=gs_v(p), in1=ub_v(p), op=ALU.mult),
                    reads=[gs[p], ub[p]], writes=[hid[m]])
            for m in range(KD):
                ps, pbuf = mm_tile("down", m, [(hid_v(j), hid[j]) for j in range(64)])
                op("V", lambda h, m=m, ps=ps: h.tensor_tensor(
                    out=xres_v(m), in0=ps[:, :], in1=xres_v(m), op=ALU.add),
                    reads=[pbuf, xres[m]], writes=[xres[m]])

            if u == 0:
                tap("d_p5", xres_v, xres, rows=KD)

            ffn_phase()
            for dd in range(KD):
                op("Q", lambda h, dd=dd, t0=t0: h.dma_start(
                    out=out_fm[dd * 128:(dd + 1) * 128, t0:t0 + T], in_=xres_v(dd)),
                    reads=[xres[dd]], queue="qQ")
            if u + 1 < NCHUNK:
                for dd in range(KD):
                    op("Q", lambda h, dd=dd, u=u: h.dma_start(
                        out=xres_v(dd), in_=x_fm[dd * 128:(dd + 1) * 128, (u + 1) * T:(u + 2) * T]),
                        writes=[xres[dd]], queue="qQ")
            sc.barrier()

        finals = {q: sc.cnt[q] for q in QUEUES if sc.cnt[q] > 0}

        def final_thunk(h, sc=sc, finals=finals):
            for q, v in finals.items():
                h.wait_ge(sc.sems[q], v)

        sc.thunks["Q"].append(final_thunk)

        with nc.Block() as block:
            @block.sync
            def _(sync):
                sc.run("Q", sync)

            @block.tensor
            def _(tensor):
                sc.run("T", tensor)

            @block.vector
            def _(vector):
                sc.run("V", vector)

            @block.scalar
            def _(scalar):
                sc.run("S", scalar)

            @block.gpsimd
            def _(gpsimd):
                sc.run("G", gpsimd)

    return nc





def prep_in_maps(L0_attn_hidden_states, L0_attn_in_proj_w, L0_attn_conv_w, L0_attn_conv_b,
                 L0_attn_dt_proj_w, L0_attn_dt_proj_b, L0_attn_A_log, L0_attn_D_param,
                 L0_attn_out_proj_w, L0_attn_ln1_w, L0_attn_ln2_w, L0_attn_gate_w,
                 L0_attn_up_w, L0_attn_down_w, L0_ffn_up_weight, L0_ffn_down_weight,
                 L0_ffn_up_scale, L0_ffn_down_scale, L0_ffn_output_scale):
    f32 = np.float32

    def fm_cols(v, ncols):
        return np.ascontiguousarray(np.asarray(v, f32).reshape(ncols, 128).T)

    x = np.asarray(L0_attn_hidden_states, f32)
    x_fm = np.ascontiguousarray(x.transpose(0, 2, 1))

    wT = {
        "inproj": np.ascontiguousarray(np.asarray(L0_attn_in_proj_w, f32).T),
        "dtproj": np.ascontiguousarray(np.asarray(L0_attn_dt_proj_w, f32).T),
        "outproj": np.ascontiguousarray(np.asarray(L0_attn_out_proj_w, f32).T),
        "gate": np.ascontiguousarray(np.asarray(L0_attn_gate_w, f32).T),
        "up": np.ascontiguousarray(np.asarray(L0_attn_up_w, f32).T),
        "down": np.ascontiguousarray(np.asarray(L0_attn_down_w, f32).T),
        "fup": np.ascontiguousarray(np.asarray(L0_ffn_up_weight, f32).T),
        "fdown": np.ascontiguousarray(np.asarray(L0_ffn_down_weight, f32).T),
    }
    pars = {
        "alog": np.ascontiguousarray(
            np.asarray(L0_attn_A_log, f32).reshape(32, 128, DST).transpose(1, 0, 2).reshape(128, 32 * DST)),
        "convw": np.ascontiguousarray(
            np.asarray(L0_attn_conv_w, f32).reshape(32, 128, DCONV).transpose(1, 0, 2).reshape(128, 32 * DCONV)),
        "convb": fm_cols(L0_attn_conv_b, 32),
        "dtb": fm_cols(L0_attn_dt_proj_b, 32),
        "dpar": fm_cols(L0_attn_D_param, 32),
        "ln1": fm_cols(L0_attn_ln1_w, KD),
        "ln2": fm_cols(L0_attn_ln2_w, KD),
        "fus": fm_cols(L0_ffn_up_scale, 64),
        "fds": fm_cols(L0_ffn_down_scale, KD),
        "oscale": np.asarray(L0_ffn_output_scale, f32).reshape(1, 1),
        "ones_col": np.ones((128, 1), f32),
        "ones_row": np.ones((1, 128), f32),
    }

    in_maps = []
    for b in range(8):
        m = {"x_fm": x_fm[b]}
        for n in WSPEC:
            m["w_" + n] = wT[n]
        m.update(pars)
        in_maps.append(m)
    return in_maps


def kernel(**inputs):
    in_maps = prep_in_maps(**inputs)
    nc = build_program()
    res = run_bass_kernel_spmd(nc, in_maps, core_ids=list(range(8)))
    out = np.stack([np.asarray(r["out_fm"]).T for r in res.results], axis=0)
    return np.ascontiguousarray(out)


# version 16
# speedup vs baseline: 1.0694x; 1.0694x over previous
import contextlib
import numpy as np
import ml_dtypes
import concourse.bass as bass
import concourse.mybir as mybir
from concourse.alu_op_type import AluOpType as ALU
from concourse.bass_utils import run_bass_kernel_spmd

F32 = mybir.dt.float32
BF16 = mybir.dt.bfloat16
AF = mybir.ActivationFunctionType
BF = ml_dtypes.bfloat16

D = 2048
DI = 4096
DXB = 1024
DST = 16
DCONV = 4
DTR = 128
INTER = 8192
L = 4096
T = 512
NCHUNK = L // T
KD = D // 128
EPS = 1e-5

MT_Z0, MT_X0, MT_B0, MT_C0, MT_DT = 0, 32, 40, 48, 80
DEBUG = False
ONLY_P1 = False
PARANOID = True
ELIDE_ENGINES = ()
P1B = 0

ENGS = ("T", "V", "S", "G")
QPOOL = {"qQ": 12, "qG": 8, "qS": 4}
QUEUES = tuple(f"{q}{i}" for q, n in QPOOL.items() for i in range(n))

WSPEC = {
    "inproj": (D, 10368), "dtproj": (DTR, DI), "outproj": (DI, D),
    "gate": (D, INTER), "up": (D, INTER), "down": (INTER, D),
    "fup": (D, INTER), "fdown": (INTER, D),
}
SIGN_W = ("fup", "fdown")


class Buf:
    __slots__ = ("w", "r", "wl")

    def __init__(self):
        self.w = None
        self.r = []
        self.wl = []


class Sched:
    def __init__(self):
        self.thunks = {e: [] for e in ENGS + ("Q",)}
        self.cnt = {s: 0 for s in ENGS + QUEUES}
        self.qissue = {q: 0 for q in QPOOL}
        self.waited = {e: {} for e in ENGS + ("Q",)}
        self.sems = None
        self.aliased = []

    def _need(self, elide, tok, need):
        sem, val = tok
        if sem == elide:
            return
        if need.get(sem, 0) < val:
            need[sem] = val

    def op(self, e, build, reads=(), writes=(), queue=None, wwrites=(), sreads=()):
        issuer = "Q" if e == "Q" else e
        elide = issuer if (queue is None and (issuer in ELIDE_ENGINES or not PARANOID)) else None
        need = {}
        for b in sreads:
            if b.w:
                self._need(None, b.w, need)
            for dep in b.wl:
                self._need(None, dep, need)
        for b in reads:
            if b.w:
                self._need(elide, b.w, need)
            for dep in b.wl:
                self._need(elide, dep, need)
        for b in writes:
            if b.w:
                self._need(elide, b.w, need)
            for dep in b.wl:
                self._need(elide, dep, need)
            for dep in b.r:
                self._need(elide, dep, need)
        for b in wwrites:
            for dep in b.r:
                self._need(elide, dep, need)
        writes = list(writes) + list(wwrites)
        waits = []
        wcache = self.waited[issuer]
        for sem, val in need.items():
            if wcache.get(sem, 0) < val:
                wcache[sem] = val
                waits.append((sem, val))
        if queue is None:
            self.cnt[e] += 1
            tok = (e, self.cnt[e])
            inc = (e, 1)
        else:
            i = self.qissue[queue]
            self.qissue[queue] = i + 1
            k = QPOOL[queue]
            qsem = f"{queue}{i % k}"
            if need.get(qsem, 0) < 16 * (i // k):
                wcache2 = self.waited[issuer]
                if wcache2.get(qsem, 0) < 16 * (i // k):
                    wcache2[qsem] = 16 * (i // k)
                    waits.append((qsem, 16 * (i // k)))
            self.cnt[qsem] += 16
            tok = (qsem, self.cnt[qsem])
            inc = (qsem, 16)
        for b in reads:
            b.r.append(tok)
        for b in sreads:
            b.r.append(tok)
        nw = len(wwrites)
        for b in writes[:len(writes) - nw]:
            b.w = tok
            b.r = []
            b.wl = []
        for b in writes[len(writes) - nw:]:
            b.w = tok
            b.r = []
            b.wl.append(tok)

        def thunk(h, waits=waits, build=build, inc=inc, self=self):
            for (s, v) in waits:
                h.wait_ge(self.sems[s], v)
            build(h).then_inc(self.sems[inc[0]], inc[1])

        self.thunks[issuer].append(thunk)

    def barrier(self):
        snap = dict(self.cnt)
        qsems = [s for s in QUEUES if not s.startswith("qQ")]
        for e in ENGS:
            waits = []
            wcache = self.waited[e]
            for s in ENGS + tuple(qsems):
                if s != e and wcache.get(s, 0) < snap[s]:
                    wcache[s] = snap[s]
                    waits.append((s, snap[s]))
            self.cnt[e] += 1

            def thunk(h, waits=waits, e=e, self=self):
                for (s, v) in waits:
                    h.wait_ge(self.sems[s], v)
                h.drain().then_inc(self.sems[e], 1)

            self.thunks[e].append(thunk)
        for b in self.aliased:
            b.w = None
            b.r = []
            b.wl = []

    def run(self, e, h):
        for t in self.thunks[e]:
            t(h)


def build_program():
    nc = bass.Bass("TRN2", target_bir_lowering=False)
    sc = Sched()
    wK = {n: i // 128 for n, (i, o) in WSPEC.items()}
    wM = {n: o // 128 for n, (i, o) in WSPEC.items()}

    x_fm = nc.dram_tensor("x_fm", [D, L], F32, kind="ExternalInput").ap()
    dw = {n: nc.dram_tensor("w_" + n, [i, o], F32, kind="ExternalInput").ap()
          for n, (i, o) in WSPEC.items()}
    dpar_in = {}
    for n, shape in [("alog", [128, 32 * DST]), ("convw", [128, 32 * DCONV]),
                     ("convb", [128, 32]), ("dtb", [128, 32]), ("dpar", [128, 32]),
                     ("ln1", [128, KD]), ("ln2", [128, KD]), ("fus", [128, 64]),
                     ("fds", [128, KD]), ("oscale", [1, 1]),
                     ("ones_col", [128, 1]), ("ones_row", [1, 128])]:
        dpar_in[n] = nc.dram_tensor(n, shape, F32, kind="ExternalInput").ap()
    out_fm = nc.dram_tensor("out_fm", [D, L], F32, kind="ExternalOutput").ap()
    wsc = {n: nc.dram_tensor("wsc_" + n, [wM[n], 128, wK[n] * 128], BF16,
                             kind="Internal").ap() for n in WSPEC}
    dbg = {}
    if DEBUG:
        for n, shape, dt_ in [("d_p1", [D, T], F32), ("d_hn", [D, T], BF16),
                              ("d_dtl", [128, T], BF16), ("d_dt0", [128, T], F32),
                              ("d_zs0", [128, T], F32), ("d_cc0", [128, T], BF16),
                              ("d_bo0", [128, T], BF16), ("d_xo0", [128, 515], BF16),
                              ("d_xrp0", [128, 515], BF16), ("d_xcv0", [128, T], F32),
                              ("d_y0", [128, T], BF16), ("d_da0", [128, T], F32),
                              ("d_hh0", [128, T], BF16), ("d_yc0", [128, T], F32),
                              ("d_p4", [D, T], F32), ("d_p5", [D, T], F32),
                              ("d_act0", [D, T], BF16), ("d_ws0", [128, 2048], BF16),
                              ("d_hid0", [128, T], BF16), ("d_pst", [1, 2 * T], F32),
                              ("d_hidall", [128, 64 * T], BF16), ("d_x0", [D, T], F32),
                              ("d_dsos", [128, KD], F32), ("d_oscol", [128, 1], F32)]:
            dbg[n] = nc.dram_tensor(n, shape, dt_, kind="ExternalOutput").ap()
    h0_dram = nc.dram_tensor("h0_dram", [KD, 128, T], F32, kind="Internal").ap()

    es = contextlib.ExitStack()
    sb = lambda name, shape, dt=F32: es.enter_context(nc.sbuf_tensor(name, shape, dt))
    mkpsum = lambda name, shape: es.enter_context(nc.psum_tensor(name, shape, F32))

    with es:
        XRES = sb("XRES", [128, KD * T])
        ACTIN = sb("ACTIN", [128, KD * T], BF16)
        WS = sb("WS", [128, 5 * 2048], BF16)
        DTW = sb("DTW", [128, DI], BF16)
        DTL = sb("DTL", [128, T], BF16)
        STATE = sb("STATE", [128, 512])
        XTL = sb("XTL", [128, 8 * 4], BF16)
        ROWS = sb("ROWS", [1, 3 * T])
        ASB = sb("ASB", [128, 32 * DST])
        CWS = sb("CWS", [128, 32 * DCONV])
        CBS = sb("CBS", [128, 32])
        DTBS = sb("DTBS", [128, 32])
        DPS = sb("DPS", [128, 32])
        LN1S = sb("LN1S", [128, KD])
        LN2S = sb("LN2S", [128, KD])
        FUSS = sb("FUSS", [128, 64])
        DSOS = sb("DSOS", [128, KD])
        OSCS = sb("OSCS", [1, 1])
        EPSC = sb("EPSC", [1, 1])
        OSCOL = sb("OSCOL", [128, 1])
        ONESC = sb("ONESC", [128, 1])
        ONESR = sb("ONESR", [1, 128])
        RB = sb("RB", [128, 34816], BF16)
        RF = sb("RF", [128, 9216])
        PMM = [mkpsum(f"PMM{i}", [128, T]) for i in range(4)]
        PST0 = mkpsum("PST0", [1, T])
        PST1 = mkpsum("PST1", [1, T])
        PB0 = mkpsum("PB0", [128, T])
        PB1 = mkpsum("PB1", [128, T])
        sems = {s: es.enter_context(nc.semaphore("sem_" + s)) for s in ENGS + QUEUES}
        sc.sems = sems

        xres = [Buf() for _ in range(KD)]
        actin = [Buf() for _ in range(KD)]
        wslot = [Buf() for _ in range(5)]
        bDTW, bDTL = Buf(), Buf()
        bSTATE = [Buf() for _ in range(512)]
        bXTL = [Buf() for _ in range(8)]
        rows = [Buf() for _ in range(3)]
        bpmm = [Buf() for _ in range(4)]
        bpst = [Buf(), Buf()]
        bpb = [Buf(), Buf()]
        wscb = {n: Buf() for n in WSPEC}
        h0b = [Buf() for _ in range(KD)]
        consts = Buf()

        def albuf(n):
            bl = [Buf() for _ in range(n)]
            sc.aliased.extend(bl)
            return bl

        hid = albuf(64)
        ybuf = albuf(32)
        xo = albuf(8)
        b4 = albuf(2); cc = albuf(2); bo = albuf(2); xrp = albuf(2)
        da = albuf(4); dbx = albuf(4); hh = albuf(2); hcp = albuf(2)
        cn = albuf(4); bn = albuf(4); gs = albuf(2); ub = albuf(2)
        zs = albuf(2); dtc = albuf(2); dtxc = albuf(2); xcv = albuf(2)
        cacc = albuf(2); spb = albuf(2); tp = albuf(2); h0t = albuf(2)
        hd = albuf(2); sq = albuf(2); yc = albuf(2)
        prew, preb = albuf(1), albuf(1)

        xres_v = lambda dd: XRES[:, dd * T:(dd + 1) * T]
        actin_v = lambda dd: ACTIN[:, dd * T:(dd + 1) * T]
        hid_v = lambda m: RB[:, m * T:(m + 1) * T]
        y_v = lambda c: RB[:, c * T:(c + 1) * T]
        XO0 = 16384
        xo_v = lambda g: RB[:, XO0 + g * 528: XO0 + g * 528 + 515]
        XRP0 = XO0 + 8 * 528
        xrp_v = lambda p: RB[:, XRP0 + p * 528: XRP0 + p * 528 + 515]
        B40 = XRP0 + 2 * 528
        b4_v = lambda p: RB[:, B40 + p * T: B40 + (p + 1) * T]
        CC0 = B40 + 2 * T
        cc_v = lambda p: RB[:, CC0 + p * T: CC0 + (p + 1) * T]
        BO0 = CC0 + 2 * T
        bo_v = lambda p: RB[:, BO0 + p * T: BO0 + (p + 1) * T]
        BN0 = BO0 + 2 * T
        bn_v = lambda p: RB[:, BN0 + p * T: BN0 + (p + 1) * T]
        CN0 = BN0 + 4 * T
        cn_v = lambda p: RB[:, CN0 + p * T: CN0 + (p + 1) * T]
        DBX0 = CN0 + 4 * T
        dbx_v = lambda p: RB[:, DBX0 + p * T: DBX0 + (p + 1) * T]
        HH0 = DBX0 + 4 * T
        hh_v = lambda p: RB[:, HH0 + p * T: HH0 + (p + 1) * T]
        HCP0 = HH0 + 2 * T
        hcp_v = lambda p: RB[:, HCP0 + p * T: HCP0 + (p + 1) * T]
        GS0 = 32768
        gs_v = lambda p: RB[:, GS0 + p * T: GS0 + (p + 1) * T]
        UB0 = GS0 + 2 * T
        ub_v = lambda p: RB[:, UB0 + p * T: UB0 + (p + 1) * T]
        preb_v = RB[:, 0:8192]
        zs_v = lambda p: RF[:, p * T:(p + 1) * T]
        dt_v = lambda p: RF[:, 1024 + p * T: 1024 + (p + 1) * T]
        dtx_v = lambda p: RF[:, 2048 + p * T: 2048 + (p + 1) * T]
        xcv_v = lambda p: RF[:, 3072 + p * T: 3072 + (p + 1) * T]
        cacc_v = lambda p: RF[:, 4096 + p * T: 4096 + (p + 1) * T]
        da_v = lambda p: RF[:, 5120 + p * T: 5120 + (p + 1) * T]
        yc_v = lambda p: RF[:, 7168 + p * T: 7168 + (p + 1) * T]
        sp_v = lambda p: RF[:, 8192 + p * T: 8192 + (p + 1) * T]
        tp_v = lambda p: RF[:, p * T:(p + 1) * T]
        sq_v = lambda p: RF[:, 1024 + p * T: 1024 + (p + 1) * T]
        h0t_v = lambda p: RF[:, 2048 + p * T: 2048 + (p + 1) * T]
        hd_v = lambda p: RF[:, 3072 + p * T: 3072 + (p + 1) * T]
        pre_v = RF[:, 0:8192]
        row_mean = ROWS[0:1, 0:T]
        row_rstd = ROWS[0:1, T:2 * T]
        row_tmp = ROWS[0:1, 2 * T:3 * T]

        op = sc.op
        shuf_masks = [[(i // 16) * 16 + n for i in range(32)] for n in range(DST)]

        for n, dst, bufw in [
            ("alog", ASB, consts), ("convw", CWS, consts), ("convb", CBS, consts),
            ("dtb", DTBS, consts), ("dpar", DPS, consts), ("ln1", LN1S, consts),
            ("ln2", LN2S, consts), ("fus", FUSS, consts), ("fds", DSOS, consts),
            ("oscale", OSCS, consts), ("ones_col", ONESC, consts),
            ("ones_row", ONESR, consts),
        ]:
            op("Q", lambda h, dst=dst, src=dpar_in[n]: h.dma_start(out=dst[:, :], in_=src[:, :]),
               wwrites=[consts], queue="qQ")
        op("S", lambda h: h.activation(out=ASB[:, :], in_=ASB[:, :], func=AF.Exp),
           reads=[consts], writes=[consts])
        op("V", lambda h: h.tensor_scalar(out=ASB[:, :], in0=ASB[:, :], scalar1=-1.0,
                                          scalar2=None, op0=ALU.mult),
           reads=[consts], writes=[consts])
        op("T", lambda h: h.matmul(PB0[:, 0:1], ONESR[0:1, :], OSCS[0:1, 0:1],
                                   start=True, stop=True),
           reads=[consts], writes=[bpb[0]])
        op("V", lambda h: h.tensor_copy(out=OSCOL[:, :], in_=PB0[:, 0:1]),
           reads=[bpb[0]], wwrites=[consts])
        op("V", lambda h: h.tensor_scalar(out=DSOS[:, :], in0=DSOS[:, :],
                                          scalar1=OSCOL[:, 0:1], scalar2=0.5,
                                          op0=ALU.mult, op1=ALU.mult),
           sreads=[consts], writes=[consts])
        if DEBUG:
            op("Q", lambda h: h.dma_start(out=dbg["d_dsos"][:, :], in_=DSOS[:, :]),
               reads=[consts], queue="qQ")
            op("Q", lambda h: h.dma_start(out=dbg["d_oscol"][:, :], in_=OSCOL[:, :]),
               reads=[consts], queue="qQ")
        op("V", lambda h: h.memset(STATE[:, :], 0.0), writes=list(bSTATE))
        op("V", lambda h: h.memset(EPSC[:, :], EPS), wwrites=[consts])
        op("V", lambda h: h.memset(XTL[:, :], 0.0), writes=list(bXTL))
        for dd in range(KD):
            op("Q", lambda h, dd=dd: h.dma_start(out=xres_v(dd), in_=x_fm[dd * 128:(dd + 1) * 128, 0:T]),
               writes=[xres[dd]], queue="qQ")

        for name in ("fup", "fdown", "inproj", "dtproj", "outproj", "gate", "up", "down"):
            K, M = wK[name], wM[name]
            G = max(1, min(M, 8192 // (K * 128)))
            for m0 in range(0, M, G):
                g_n = min(G, M - m0)
                w_ = dw[name]
                for k in range(K):
                    op("Q", lambda h, k=k, m0=m0, g_n=g_n, w_=w_, K=K:
                       h.dma_start(out=pre_v[:, k * g_n * 128:(k + 1) * g_n * 128],
                                   in_=w_[k * 128:(k + 1) * 128, m0 * 128:(m0 + g_n) * 128]),
                       wwrites=[prew[0]], queue="qQ")
                ncols = K * g_n * 128
                if name in SIGN_W:
                    op("S", lambda h, ncols=ncols: h.activation(
                        out=preb_v[:, 0:ncols], in_=pre_v[:, 0:ncols], func=AF.Sign),
                        reads=[prew[0]], writes=[preb[0]])
                else:
                    op("V", lambda h, ncols=ncols: h.tensor_copy(
                        out=preb_v[:, 0:ncols], in_=pre_v[:, 0:ncols]),
                        reads=[prew[0]], writes=[preb[0]])
                src = preb_v[:, 0:ncols].rearrange("p (k mg cc) -> p k mg cc", k=K, mg=g_n, cc=128)
                dst = wsc[name][m0:m0 + g_n, :, :].rearrange("mg p (k cc) -> p k mg cc", k=K, cc=128)
                op("S", lambda h, src=src, dst=dst: h.dma_start(out=dst, in_=src),
                   reads=[preb[0]], writes=[wscb[name]], queue="qS")
        sc.barrier()

        ring = [0]
        pmm_ring = [0]

        def mm_tile(name, m, moving, extra_reads=()):
            K = wK[name]
            assert len(moving) == K
            pi = pmm_ring[0]
            pmm_ring[0] = (pi + 1) % 4
            k = 0
            for j in range((K + 15) // 16):
                sl = ring[0]
                ring[0] = (sl + 1) % 5
                k0 = j * 16
                kn = min(16, K - k0)
                view = WS[:, sl * 2048: sl * 2048 + kn * 128]
                src = wsc[name][m:m + 1, :, k0 * 128:(k0 + kn) * 128]
                op("Q", lambda h, view=view, src=src: h.dma_start(out=view, in_=src),
                   reads=[wscb[name]], writes=[wslot[sl]], queue="qQ")
                for kk in range(kn):
                    lhsT = view[:, kk * 128:(kk + 1) * 128]
                    rhs, rbuf = moving[k]
                    st_ = (k == 0)
                    sp_ = (k == K - 1)
                    wr = [bpmm[pi]] if (st_ or sp_) else []
                    op("T", lambda h, lhsT=lhsT, rhs=rhs, pi=pi, st_=st_, sp_=sp_:
                       h.matmul(PMM[pi][:, :], lhsT, rhs, start=st_, stop=sp_),
                       reads=[wslot[sl], rbuf] + list(extra_reads), writes=wr)
                    k += 1
            return PMM[pi], bpmm[pi]

        def actin_moving():
            return [(actin_v(dd), actin[dd]) for dd in range(KD)]

        def ln_stats(views_bufs, which=0):
            nsrc = len(views_bufs)
            for i, (v, b) in enumerate(views_bufs):
                p = i % 2
                op("S", lambda h, v=v, p=p: h.square(out=sq_v(p), in_=v), reads=[b], writes=[sq[p]])
                op("T", lambda h, v=v, st_=(i == 0), sp_=(i == nsrc - 1):
                   h.matmul(PST0[:, :], ONESC[:, 0:1], v, start=st_, stop=sp_),
                   reads=[b, consts], writes=[bpst[0]])
                op("T", lambda h, p=p, st_=(i == 0), sp_=(i == nsrc - 1):
                   h.matmul(PST1[:, :], ONESC[:, 0:1], sq_v(p), start=st_, stop=sp_),
                   reads=[sq[p], consts], writes=[bpst[1]])

        def rows_layernorm():
            op("V", lambda h: h.tensor_scalar(out=row_mean, in0=PST0[:, :], scalar1=1.0 / D,
                                              scalar2=None, op0=ALU.mult),
               reads=[bpst[0]], writes=[rows[0]])
            op("V", lambda h: h.tensor_scalar(out=row_tmp, in0=PST1[:, :], scalar1=1.0 / D,
                                              scalar2=None, op0=ALU.mult),
               reads=[bpst[1]], writes=[rows[2]])
            op("V", lambda h: h.tensor_tensor(out=row_rstd, in0=row_mean, in1=row_mean, op=ALU.mult),
               sreads=[rows[0]], writes=[rows[1]])
            op("V", lambda h: h.tensor_tensor(out=row_tmp, in0=row_tmp, in1=row_rstd, op=ALU.subtract),
               sreads=[rows[2], rows[1]], writes=[rows[2]])
            op("S", lambda h: h.activation(out=row_rstd, in_=row_tmp, func=AF.Sqrt, bias=EPSC[0:1, 0:1]),
               sreads=[rows[2], consts], writes=[rows[1]])
            op("V", lambda h: h.reciprocal(out=row_rstd, in_=row_rstd),
               sreads=[rows[1]], writes=[rows[1]])

        def rows_rms_only():
            op("V", lambda h: h.tensor_scalar(out=row_tmp, in0=PST1[:, :], scalar1=1.0 / D,
                                              scalar2=None, op0=ALU.mult),
               reads=[bpst[1]], writes=[rows[2]])
            op("S", lambda h: h.activation(out=row_rstd, in_=row_tmp, func=AF.Sqrt, bias=EPSC[0:1, 0:1]),
               sreads=[rows[2], consts], writes=[rows[1]])
            op("V", lambda h: h.reciprocal(out=row_rstd, in_=row_rstd),
               sreads=[rows[1]], writes=[rows[1]])

        def bcast(psum_t, psum_b, row_ap, row_b):
            op("T", lambda h: h.matmul(psum_t[:, :], ONESR[0:1, :], row_ap, start=True, stop=True),
               sreads=[row_b, consts], writes=[psum_b])

        def tap(name, src_ap, src_bufs, rows=None):
            if not DEBUG or name not in dbg:
                return
            dsts = dbg[name]
            if rows is None:
                op("Q", lambda h, dsts=dsts, src_ap=src_ap: h.dma_start(out=dsts[:, :], in_=src_ap),
                   reads=list(src_bufs), queue="qQ")
            else:
                for dd in range(rows):
                    op("Q", lambda h, dd=dd, dsts=dsts: h.dma_start(
                        out=dsts[dd * 128:(dd + 1) * 128, :], in_=src_ap(dd)),
                        reads=[src_bufs[dd]], queue="qQ")

        for u in range(NCHUNK):
            t0 = u * T

            def ffn_phase():
                ln_stats([(xres_v(dd), xres[dd]) for dd in range(KD)])
                rows_layernorm()
                bcast(PB0, bpb[0], row_mean, rows[0])
                bcast(PB1, bpb[1], row_rstd, rows[1])
                if P1B & 1:
                    sc.barrier()
                for dd in range(KD):
                    p = dd % 2
                    op("V", lambda h, dd=dd, p=p: h.tensor_tensor(
                        out=tp_v(p), in0=xres_v(dd), in1=PB0[:, :], op=ALU.subtract),
                        reads=[xres[dd], bpb[0]], writes=[tp[p]])
                    op("V", lambda h, dd=dd, p=p: h.tensor_tensor(
                        out=actin_v(dd), in0=tp_v(p), in1=PB1[:, :], op=ALU.mult),
                        reads=[tp[p], bpb[1]], writes=[actin[dd]])
                if P1B & 2:
                    sc.barrier()
                if u == 0 and not ffn_phase.tapped:
                    ffn_phase.tapped = True
                    tap("d_act0", actin_v, actin, rows=KD)
                    op("V", lambda h: h.tensor_copy(out=RF[:, 7680:8704], in_=ROWS[0:1, 0:1024].rearrange("o (a t) -> o a t", a=2)) if False else h.nop(), reads=[])
                for m in range(64):
                    ps, pbuf = mm_tile("fup", m, actin_moving())
                    if u == 0 and m == 0 and ffn_phase.tapped == True:
                        ffn_phase.tapped = "done"
                        op("Q", lambda h: h.dma_start(out=dbg["d_ws0"][:, :], in_=WS[:, 0:2048]),
                           reads=[wslot[0]], queue="qQ") if DEBUG else None
                    op("S", lambda h, m=m, ps=ps: h.activation(
                        out=hid_v(m), in_=ps[:, :], func=AF.Relu, scale=FUSS[:, m:m + 1]),
                        reads=[pbuf], sreads=[consts], writes=[hid[m]])
                    if u == 0 and m == 0 and DEBUG:
                        tap("d_hid0", hid_v(0), [hid[0]])
                if P1B & 4:
                    sc.barrier()
                if u == 0 and DEBUG and ffn_phase.tapped == "done":
                    ffn_phase.tapped = "done2"
                    for mj in range(64):
                        op("Q", lambda h, mj=mj: h.dma_start(
                            out=dbg["d_hidall"][:, mj * T:(mj + 1) * T], in_=hid_v(mj)),
                            reads=[hid[mj]], queue="qQ")
                for m in range(KD):
                    ps, pbuf = mm_tile("fdown", m, [(hid_v(j), hid[j]) for j in range(64)])
                    op("V", lambda h, m=m, ps=ps: h.scalar_tensor_tensor(
                        out=xres_v(m), in0=ps[:, :], scalar=DSOS[:, m:m + 1], in1=xres_v(m),
                        op0=ALU.mult, op1=ALU.add),
                        reads=[pbuf, xres[m]], sreads=[consts], writes=[xres[m]])

            if u == 0:
                tap("d_x0", xres_v, xres, rows=KD)
            ffn_phase.tapped = False
            ffn_phase()
            if u == 0:
                tap("d_p1", xres_v, xres, rows=KD)
            if ONLY_P1:
                for dd in range(KD):
                    op("Q", lambda h, dd=dd, t0=t0: h.dma_start(
                        out=out_fm[dd * 128:(dd + 1) * 128, t0:t0 + T], in_=xres_v(dd)),
                        reads=[xres[dd]], queue="qQ")
                if u + 1 < NCHUNK:
                    for dd in range(KD):
                        op("Q", lambda h, dd=dd, u=u: h.dma_start(
                            out=xres_v(dd), in_=x_fm[dd * 128:(dd + 1) * 128, (u + 1) * T:(u + 2) * T]),
                            writes=[xres[dd]], queue="qQ")
                sc.barrier()
                continue

            ln_stats([(xres_v(dd), xres[dd]) for dd in range(KD)])
            rows_layernorm()
            bcast(PB0, bpb[0], row_mean, rows[0])
            bcast(PB1, bpb[1], row_rstd, rows[1])
            for dd in range(KD):
                p = dd % 2
                op("V", lambda h, dd=dd, p=p: h.tensor_tensor(
                    out=tp_v(p), in0=xres_v(dd), in1=PB0[:, :], op=ALU.subtract),
                    reads=[xres[dd], bpb[0]], writes=[tp[p]])
                op("V", lambda h, p=p: h.tensor_tensor(
                    out=h0t_v(p), in0=tp_v(p), in1=PB1[:, :], op=ALU.mult),
                    reads=[tp[p], bpb[1]], writes=[h0t[p]])
                op("S", lambda h, p=p, dd=dd: h.dma_start(
                    out=h0_dram[dd:dd + 1, :, :], in_=h0t_v(p)),
                    reads=[h0t[p]], writes=[h0b[dd]], queue="qS")
                op("S", lambda h, p=p: h.square(out=sq_v(p), in_=h0t_v(p)),
                   reads=[h0t[p]], writes=[sq[p]])
                op("T", lambda h, p=p, st_=(dd == 0), sp_=(dd == KD - 1):
                   h.matmul(PST1[:, :], ONESC[:, 0:1], sq_v(p), start=st_, stop=sp_),
                   reads=[sq[p], consts], writes=[bpst[1]])
            op("V", lambda h: h.tensor_scalar(out=row_tmp, in0=PST1[:, :], scalar1=1.0 / D,
                                              scalar2=None, op0=ALU.mult),
               reads=[bpst[1]], writes=[rows[2]])
            op("S", lambda h: h.activation(out=row_tmp, in_=row_tmp, func=AF.Sqrt, bias=EPSC[0:1, 0:1]),
               sreads=[rows[2], consts], writes=[rows[2]])
            op("V", lambda h: h.reciprocal(out=row_tmp, in_=row_tmp),
               sreads=[rows[2]], writes=[rows[2]])
            op("V", lambda h: h.tensor_tensor(out=row_tmp, in0=row_tmp, in1=row_rstd, op=ALU.mult),
               sreads=[rows[2], rows[1]], writes=[rows[2]])
            bcast(PB1, bpb[1], row_tmp, rows[2])
            for dd in range(KD):
                p = dd % 2
                op("V", lambda h, dd=dd, p=p: h.tensor_tensor(
                    out=tp_v(p), in0=xres_v(dd), in1=PB0[:, :], op=ALU.subtract),
                    reads=[xres[dd], bpb[0]], writes=[tp[p]])
                op("V", lambda h, dd=dd, p=p: h.scalar_tensor_tensor(
                    out=actin_v(dd), in0=PB1[:, :], scalar=LN1S[:, dd:dd + 1], in1=tp_v(p),
                    op0=ALU.mult, op1=ALU.mult),
                    reads=[bpb[1], tp[p]], sreads=[consts], writes=[actin[dd]])

            if u == 0:
                tap("d_hn", actin_v, actin, rows=KD)
            sc.barrier()

            op("Q", lambda h: h.dma_start(
                out=DTW[:, :].rearrange("p (mg cc) -> p mg cc", mg=32, cc=128),
                in_=wsc["dtproj"].rearrange("mg p cc -> p mg cc")),
                reads=[wscb["dtproj"]], writes=[bDTW], queue="qQ")
            ps, pbuf = mm_tile("inproj", MT_DT, actin_moving())
            op("S", lambda h, ps=ps: h.activation(out=DTL[:, :], in_=ps[:, :], func=AF.Copy),
               reads=[pbuf], writes=[bDTL])
            if u == 0:
                tap("d_dtl", DTL[:, :], [bDTL])

            for c in range(32):
                pc = c % 2
                g = c // 4
                if c % 4 == 0:
                    pg = g % 2
                    ps, pbuf = mm_tile("inproj", MT_X0 + g, actin_moving())
                    op("S", lambda h, g=g, ps=ps: h.activation(
                        out=RB[:, XO0 + g * 528 + 3: XO0 + g * 528 + 515],
                        in_=ps[:, :], func=AF.Copy),
                        reads=[pbuf], writes=[xo[g]])
                    op("V", lambda h, g=g: h.tensor_copy(
                        out=RB[:, XO0 + g * 528: XO0 + g * 528 + 3],
                        in_=XTL[:, g * 4: g * 4 + 3]),
                        reads=[bXTL[g]], writes=[xo[g]])
                    ps, pbuf = mm_tile("inproj", MT_B0 + g, actin_moving())
                    op("S", lambda h, pg=pg, ps=ps: h.activation(
                        out=bo_v(pg), in_=ps[:, :], func=AF.Copy),
                        reads=[pbuf], writes=[bo[pg]])
                pg = g % 2
                ps, pbuf = mm_tile("inproj", MT_Z0 + c, actin_moving())
                op("S", lambda h, pc=pc, ps=ps: h.activation(
                    out=zs_v(pc), in_=ps[:, :], func=AF.Silu),
                    reads=[pbuf], writes=[zs[pc]])
                ps, pbuf = mm_tile("inproj", MT_C0 + c, actin_moving())
                op("S", lambda h, pc=pc, ps=ps: h.activation(
                    out=cc_v(pc), in_=ps[:, :], func=AF.Copy),
                    reads=[pbuf], writes=[cc[pc]])
                pi = pmm_ring[0]
                pmm_ring[0] = (pi + 1) % 4
                op("T", lambda h, c=c, pi=pi: h.matmul(
                    PMM[pi][:, :], DTW[:, c * 128:(c + 1) * 128], DTL[:, :],
                    start=True, stop=True),
                    reads=[bDTW, bDTL], writes=[bpmm[pi]])
                op("S", lambda h, c=c, pi=pi: h.activation(
                    out=sp_v(0), in_=PMM[pi][:, :], func=AF.Abs, bias=DTBS[:, c:c + 1]),
                    reads=[bpmm[pi]], sreads=[consts], writes=[spb[0]])
                op("S", lambda h: h.activation(out=sp_v(0), in_=sp_v(0), func=AF.Exp, scale=-1.0),
                   reads=[spb[0]], writes=[spb[0]])
                op("S", lambda h: h.activation(out=sp_v(0), in_=sp_v(0), func=AF.Ln, bias=1.0),
                   reads=[spb[0]], writes=[spb[0]])
                op("S", lambda h, c=c, pi=pi: h.activation(
                    out=sp_v(1), in_=PMM[pi][:, :], func=AF.Relu, bias=DTBS[:, c:c + 1]),
                    reads=[bpmm[pi]], sreads=[consts], writes=[spb[1]])
                op("V", lambda h, pc=pc: h.tensor_tensor(
                    out=dt_v(pc), in0=sp_v(0), in1=sp_v(1), op=ALU.add),
                    reads=[spb[0], spb[1]], writes=[dtc[pc]])
                if u == 0 and c == 0:
                    tap("d_dt0", dt_v(pc), [dtc[pc]])
                    tap("d_zs0", zs_v(pc), [zs[pc]])
                    tap("d_cc0", cc_v(pc), [cc[pc]])
                    tap("d_bo0", bo_v(pg), [bo[pg]])
                    tap("d_xo0", xo_v(0), [xo[0]])
                for b_ in range(4):
                    for a_ in range(2):
                        dsl = (64 * a_ + 16 * b_, 64 * a_ + 16 * b_ + 16)
                        ssl = (32 * (c % 4) + 16 * a_, 32 * (c % 4) + 16 * a_ + 16)
                        op("Q", lambda h, dsl=dsl, ssl=ssl, g=g, pc=pc: h.dma_start(
                            out=xrp_v(pc)[dsl[0]:dsl[1], :],
                            in_=xo_v(g)[ssl[0]:ssl[1], :]),
                            reads=[xo[g]], writes=[xrp[pc]], queue="qG")
                        op("Q", lambda h, dsl=dsl, ssl=ssl, pg=pg, pc=pc: h.dma_start(
                            out=b4_v(pc)[dsl[0]:dsl[1], 0:512],
                            in_=bo_v(pg)[ssl[0]:ssl[1], 0:512]),
                            reads=[bo[pg]], writes=[b4[pc]], queue="qG")
                op("V", lambda h, c=c, pc=pc: h.tensor_scalar(
                    out=cacc_v(0), in0=xrp_v(pc)[:, 0:T], scalar1=CWS[:, c * 4: c * 4 + 1],
                    scalar2=None, op0=ALU.mult),
                    reads=[xrp[pc]], sreads=[consts], writes=[cacc[0]])
                for j in (1, 2, 3):
                    src_p, dst_p = (j + 1) % 2, j % 2
                    op("V", lambda h, c=c, j=j, pc=pc, src_p=src_p, dst_p=dst_p:
                       h.scalar_tensor_tensor(
                           out=cacc_v(dst_p), in0=xrp_v(pc)[:, j: j + T],
                           scalar=CWS[:, c * 4 + j: c * 4 + j + 1], in1=cacc_v(src_p),
                           op0=ALU.mult, op1=ALU.add),
                       reads=[xrp[pc], cacc[src_p]], sreads=[consts], writes=[cacc[dst_p]])
                op("S", lambda h, c=c, pc=pc: h.activation(
                    out=xcv_v(pc), in_=cacc_v(1), func=AF.Silu, bias=CBS[:, c:c + 1]),
                    reads=[cacc[1]], sreads=[consts], writes=[xcv[pc]])
                if u == 0 and c == 0:
                    tap("d_xrp0", xrp_v(pc), [xrp[pc]])
                    tap("d_xcv0", xcv_v(pc), [xcv[pc]])
                op("G", lambda h, pc=pc: h.tensor_tensor(
                    out=dtx_v(pc), in0=dt_v(pc), in1=xcv_v(pc), op=ALU.mult),
                    reads=[dtc[pc], xcv[pc]], writes=[dtxc[pc]])
                LA = 2
                for j in range(DST + LA):
                    if j < DST:
                        n, p4 = j, j % 4
                        col = c * 16 + n
                        op("S", lambda h, pc=pc, p4=p4, col=col: h.activation(
                            out=da_v(p4), in_=dt_v(pc), func=AF.Exp, scale=ASB[:, col:col + 1]),
                            reads=[dtc[pc]], sreads=[consts], writes=[da[p4]])
                        op("V", lambda h, pc=pc, p4=p4, n=n: h.stream_shuffle(
                            out=bn_v(p4), in_=b4_v(pc), mask=shuf_masks[n]),
                            reads=[b4[pc]], writes=[bn[p4]])
                        op("V", lambda h, pc=pc, p4=p4, n=n: h.stream_shuffle(
                            out=cn_v(p4), in_=cc_v(pc), mask=shuf_masks[n]),
                            reads=[cc[pc]], writes=[cn[p4]])
                        op("G", lambda h, pc=pc, p4=p4: h.tensor_tensor(
                            out=dbx_v(p4), in0=dtx_v(pc), in1=bn_v(p4), op=ALU.mult),
                            reads=[dtxc[pc], bn[p4]], writes=[dbx[p4]])
                    n = j - LA
                    if 0 <= n < DST:
                        pn, p4 = n % 2, n % 4
                        col = c * 16 + n
                        op("V", lambda h, pn=pn, p4=p4, col=col: h.tensor_tensor_scan(
                            out=hh_v(pn), data0=da_v(p4), data1=dbx_v(p4),
                            initial=STATE[:, col:col + 1], op0=ALU.mult, op1=ALU.add),
                            reads=[da[p4], dbx[p4]], sreads=[bSTATE[col]], writes=[hh[pn]])
                        op("S", lambda h, pn=pn, col=col: h.copy(
                            out=STATE[:, col:col + 1], in_=hh_v(pn)[:, T - 1:T]),
                            reads=[hh[pn]], writes=[bSTATE[col]])
                        op("V", lambda h, pn=pn, p4=p4: h.tensor_tensor(
                            out=hcp_v(pn), in0=hh_v(pn), in1=cn_v(p4), op=ALU.mult),
                            reads=[hh[pn], cn[p4]], writes=[hcp[pn]])
                        if n == 0:
                            op("G", lambda h, pn=pn, pc=pc: h.tensor_copy(
                                out=yc_v(pc), in_=hcp_v(pn)),
                                reads=[hcp[pn]], writes=[yc[pc]])
                        else:
                            op("G", lambda h, pn=pn, pc=pc: h.tensor_tensor(
                                out=yc_v(pc), in0=hcp_v(pn), in1=yc_v(pc), op=ALU.add),
                                reads=[hcp[pn], yc[pc]], writes=[yc[pc]])
                op("V", lambda h, c=c, pc=pc: h.scalar_tensor_tensor(
                    out=yc_v(pc), in0=xcv_v(pc), scalar=DPS[:, c:c + 1], in1=yc_v(pc),
                    op0=ALU.mult, op1=ALU.add),
                    reads=[xcv[pc], yc[pc]], sreads=[consts], writes=[yc[pc]])
                if u == 0 and c == 0:
                    tap("d_yc0", yc_v(pc), [yc[pc]])
                op("V", lambda h, c=c, pc=pc: h.tensor_tensor(
                    out=y_v(c), in0=yc_v(pc), in1=zs_v(pc), op=ALU.mult),
                    reads=[yc[pc], zs[pc]], writes=[ybuf[c]])
                if u == 0 and c == 0:
                    tap("d_y0", y_v(c), [ybuf[c]])
            for g in range(8):
                op("V", lambda h, g=g: h.tensor_copy(
                    out=XTL[:, g * 4: g * 4 + 3],
                    in_=RB[:, XO0 + g * 528 + 512: XO0 + g * 528 + 515]),
                    reads=[xo[g]], writes=[bXTL[g]])

            sc.barrier()

            for m in range(KD):
                p = m % 2
                op("S", lambda h, m=m, p=p: h.dma_start(
                    out=h0t_v(p), in_=h0_dram[m:m + 1, :, :]),
                    reads=[h0b[m]], writes=[h0t[p]], queue="qS")
                ps, pbuf = mm_tile("outproj", m, [(y_v(c), ybuf[c]) for c in range(32)])
                op("V", lambda h, p=p, ps=ps: h.tensor_tensor(
                    out=hd_v(p), in0=ps[:, :], in1=h0t_v(p), op=ALU.add),
                    reads=[pbuf, h0t[p]], writes=[hd[p]])
                op("V", lambda h, m=m, p=p: h.tensor_tensor(
                    out=xres_v(m), in0=hd_v(p), in1=xres_v(m), op=ALU.add),
                    reads=[hd[p], xres[m]], writes=[xres[m]])
                op("S", lambda h, m=m, p=p: h.activation(
                    out=actin_v(m), in_=hd_v(p), func=AF.Copy),
                    reads=[hd[p]], writes=[actin[m]])
                op("S", lambda h, p=p: h.square(out=sq_v(p), in_=hd_v(p)),
                   reads=[hd[p]], writes=[sq[p]])
                op("T", lambda h, p=p, st_=(m == 0), sp_=(m == KD - 1):
                   h.matmul(PST1[:, :], ONESC[:, 0:1], sq_v(p), start=st_, stop=sp_),
                   reads=[sq[p], consts], writes=[bpst[1]])
            rows_rms_only()
            bcast(PB0, bpb[0], row_rstd, rows[1])
            for dd in range(KD):
                op("V", lambda h, dd=dd: h.scalar_tensor_tensor(
                    out=actin_v(dd), in0=PB0[:, :], scalar=LN2S[:, dd:dd + 1], in1=actin_v(dd),
                    op0=ALU.mult, op1=ALU.mult),
                    reads=[bpb[0], actin[dd]], sreads=[consts], writes=[actin[dd]])

            if u == 0:
                tap("d_p4", xres_v, xres, rows=KD)
            sc.barrier()

            for m in range(64):
                p = m % 2
                psg, pbg = mm_tile("gate", m, actin_moving())
                op("S", lambda h, p=p, psg=psg: h.activation(
                    out=gs_v(p), in_=psg[:, :], func=AF.Silu),
                    reads=[pbg], writes=[gs[p]])
                psu, pbu = mm_tile("up", m, actin_moving())
                op("S", lambda h, p=p, psu=psu: h.activation(
                    out=ub_v(p), in_=psu[:, :], func=AF.Copy),
                    reads=[pbu], writes=[ub[p]])
                op("V", lambda h, m=m, p=p: h.tensor_tensor(
                    out=hid_v(m), in0=gs_v(p), in1=ub_v(p), op=ALU.mult),
                    reads=[gs[p], ub[p]], writes=[hid[m]])
            for m in range(KD):
                ps, pbuf = mm_tile("down", m, [(hid_v(j), hid[j]) for j in range(64)])
                op("V", lambda h, m=m, ps=ps: h.tensor_tensor(
                    out=xres_v(m), in0=ps[:, :], in1=xres_v(m), op=ALU.add),
                    reads=[pbuf, xres[m]], writes=[xres[m]])

            if u == 0:
                tap("d_p5", xres_v, xres, rows=KD)

            ffn_phase()
            for dd in range(KD):
                op("Q", lambda h, dd=dd, t0=t0: h.dma_start(
                    out=out_fm[dd * 128:(dd + 1) * 128, t0:t0 + T], in_=xres_v(dd)),
                    reads=[xres[dd]], queue="qQ")
            if u + 1 < NCHUNK:
                for dd in range(KD):
                    op("Q", lambda h, dd=dd, u=u: h.dma_start(
                        out=xres_v(dd), in_=x_fm[dd * 128:(dd + 1) * 128, (u + 1) * T:(u + 2) * T]),
                        writes=[xres[dd]], queue="qQ")
            sc.barrier()

        finals = {q: sc.cnt[q] for q in QUEUES if sc.cnt[q] > 0}

        def final_thunk(h, sc=sc, finals=finals):
            for q, v in finals.items():
                h.wait_ge(sc.sems[q], v)

        sc.thunks["Q"].append(final_thunk)

        with nc.Block() as block:
            @block.sync
            def _(sync):
                sc.run("Q", sync)

            @block.tensor
            def _(tensor):
                sc.run("T", tensor)

            @block.vector
            def _(vector):
                sc.run("V", vector)

            @block.scalar
            def _(scalar):
                sc.run("S", scalar)

            @block.gpsimd
            def _(gpsimd):
                sc.run("G", gpsimd)

    return nc





def prep_in_maps(L0_attn_hidden_states, L0_attn_in_proj_w, L0_attn_conv_w, L0_attn_conv_b,
                 L0_attn_dt_proj_w, L0_attn_dt_proj_b, L0_attn_A_log, L0_attn_D_param,
                 L0_attn_out_proj_w, L0_attn_ln1_w, L0_attn_ln2_w, L0_attn_gate_w,
                 L0_attn_up_w, L0_attn_down_w, L0_ffn_up_weight, L0_ffn_down_weight,
                 L0_ffn_up_scale, L0_ffn_down_scale, L0_ffn_output_scale):
    f32 = np.float32

    def fm_cols(v, ncols):
        return np.ascontiguousarray(np.asarray(v, f32).reshape(ncols, 128).T)

    x = np.asarray(L0_attn_hidden_states, f32)
    x_fm = np.ascontiguousarray(x.transpose(0, 2, 1))

    wT = {
        "inproj": np.ascontiguousarray(np.asarray(L0_attn_in_proj_w, f32).T),
        "dtproj": np.ascontiguousarray(np.asarray(L0_attn_dt_proj_w, f32).T),
        "outproj": np.ascontiguousarray(np.asarray(L0_attn_out_proj_w, f32).T),
        "gate": np.ascontiguousarray(np.asarray(L0_attn_gate_w, f32).T),
        "up": np.ascontiguousarray(np.asarray(L0_attn_up_w, f32).T),
        "down": np.ascontiguousarray(np.asarray(L0_attn_down_w, f32).T),
        "fup": np.ascontiguousarray(np.asarray(L0_ffn_up_weight, f32).T),
        "fdown": np.ascontiguousarray(np.asarray(L0_ffn_down_weight, f32).T),
    }
    pars = {
        "alog": np.ascontiguousarray(
            np.asarray(L0_attn_A_log, f32).reshape(32, 128, DST).transpose(1, 0, 2).reshape(128, 32 * DST)),
        "convw": np.ascontiguousarray(
            np.asarray(L0_attn_conv_w, f32).reshape(32, 128, DCONV).transpose(1, 0, 2).reshape(128, 32 * DCONV)),
        "convb": fm_cols(L0_attn_conv_b, 32),
        "dtb": fm_cols(L0_attn_dt_proj_b, 32),
        "dpar": fm_cols(L0_attn_D_param, 32),
        "ln1": fm_cols(L0_attn_ln1_w, KD),
        "ln2": fm_cols(L0_attn_ln2_w, KD),
        "fus": fm_cols(L0_ffn_up_scale, 64),
        "fds": fm_cols(L0_ffn_down_scale, KD),
        "oscale": np.asarray(L0_ffn_output_scale, f32).reshape(1, 1),
        "ones_col": np.ones((128, 1), f32),
        "ones_row": np.ones((1, 128), f32),
    }

    in_maps = []
    for b in range(8):
        m = {"x_fm": x_fm[b]}
        for n in WSPEC:
            m["w_" + n] = wT[n]
        m.update(pars)
        in_maps.append(m)
    return in_maps


def kernel(**inputs):
    in_maps = prep_in_maps(**inputs)
    nc = build_program()
    res = run_bass_kernel_spmd(nc, in_maps, core_ids=list(range(8)))
    out = np.stack([np.asarray(r["out_fm"]).T for r in res.results], axis=0)
    return np.ascontiguousarray(out)
